# version 03 (final)
# baseline (speedup 1.0000x reference)
import numpy as np
import concourse.bass as bass
import concourse.mybir as mybir
from concourse.bass_utils import run_bass_kernel_spmd

F32 = mybir.dt.float32
AF = mybir.ActivationFunctionType
ALU = mybir.AluOpType
AX = mybir.AxisListType

D = 1024
NCH = 8
T = 4096
STRIP = 512
NSTRIP = T // STRIP
BOOM = 4096
NBOOMCH = BOOM // 128
FFND = 2816
FFNCH = 22
EPS = 1e-5
L2EPS = 1e-6
CHUNK = 128
NCHUNKS = T // CHUNK
HK = 256
NHEAD = 4
INVD = 1.0 / D
BST = 256
NBST = T // BST
ENGINES = ("sync", "vector", "scalar", "tensor", "gpsimd")
STQ = "gpsimd"

DEBUG_STAGE = None
DEBUG_S1_STRIPS = NSTRIP
DEBUG_S1_PART = 3
DEBUG_NO_SETUP = False
DEBUG_LN_PART = 5
DEBUG_FFN_PART = 3


class P:

    def __init__(self, nc, cur, eng, sems, sb, ps, env):
        self.nc = nc
        self.cur = cur
        self.eng = eng
        self.sems = sems
        self.sb = sb
        self.ps = ps
        self.env = env
        self.cnt = {}

    def bump(self, name, amt=1):
        self.cnt[name] = self.cnt.get(name, 0) + amt
        return self.cnt[name]

    def val(self, name):
        return self.cnt.get(name, 0)

    def dma(self, q, out, in_, sem=None, **kw):
        key = sem or ("d_" + q)
        n = self.bump(key, 16)
        if self.cur == q:
            self.eng.dma_start(out=out, in_=in_, **kw).then_inc(
                self.sems[key], 16)
        return n

    def op(self, ename, fn, inc=False):
        if self.cur == ename:
            fn(self.eng)
            if inc:
                self.eng.drain().then_inc(self.sems["c_" + ename], 1)
        if inc:
            return self.bump("c_" + ename, 1)
        return None

    def mark(self, ename):
        n = self.bump("c_" + ename, 1)
        if self.cur == ename:
            self.eng.drain().then_inc(self.sems["c_" + ename], 1)
        return n

    def wait(self, ename, sem, v):
        if self.cur == ename:
            self.eng.wait_ge(self.sems[sem], v)

    def wait_many(self, enames, sem, v):
        for e in enames:
            self.wait(e, sem, v)

    def barrier(self, engines=ENGINES):
        marks = {}
        for e in engines:
            if self.cur == e:
                self.eng.drain().then_inc(self.sems["c_" + e], 1)
            self.bump("c_" + e, 1)
        post = {e: self.val("c_" + e) for e in engines}
        dvals = {q: self.val(q) for q in ("d_sync", "d_gpsimd", "d_scalar", "d_s0", "d_s1")}
        for e in engines:
            for e2 in engines:
                if e2 != e:
                    self.wait(e, "c_" + e2, post[e2])
            for q, v in dvals.items():
                if v:
                    self.wait(e, q, v)


def build_program():
    nc = bass.Bass("TRN2", target_bir_lowering=False, debug=False,
                   detect_race_conditions=False)

    def ein(name, shape):
        return nc.dram_tensor(name, list(shape), F32, kind="ExternalInput").ap()

    def itn(name, shape):
        return nc.dram_tensor(name, list(shape), F32, kind="Internal").ap()

    E = {}
    E["x0"] = ein("x0", [NCH, 128, T])
    for nm in ("wqTe", "wkTe", "wvTe", "wqbTe", "wkbTe", "wvbTe", "woTe"):
        E[nm] = ein(nm, [NCH, 128, D])
    for nm in ("bq", "bk", "bv", "g_conv", "b_conv", "g_attn", "b_attn",
               "g_boom", "b_boom", "conv_b", "bu", "normw"):
        E[nm] = ein(nm, [128, NCH])
    E["cwT"] = ein("cwT", [4, 4, 256, 256])
    E["wbTe"] = ein("wbTe", [NBOOMCH, 128, D])
    E["bb"] = ein("bb", [128, NBOOMCH])
    E["wuT"] = ein("wuT", [BOOM, D])
    for nm in ("cqb", "ckb", "cvb"):
        E[nm] = ein(nm, [128, 4 * NCH])
    E["wbm"] = ein("wbm", [128, NCH * 8])
    E["wgTr"] = ein("wgTr", [D, FFND])
    E["wuTr"] = ein("wuTr", [D, FFND])
    E["wdTr"] = ein("wdTr", [FFND, D])
    E["invsc"] = ein("invsc", [128, T])
    E["stril_neg"] = ein("stril_neg", [128, 128])
    E["striu_neg"] = ein("striu_neg", [128, 128])
    E["causal_incl"] = ein("causal_incl", [128, 128])
    E["causalT_incl"] = ein("causalT_incl", [128, 128])
    E["identm"] = ein("identm", [128, 128])
    E["sel8"] = ein("sel8", [8, 8, 128])

    E["xout"] = nc.dram_tensor("xout", [NCH, 128, T], F32,
                               kind="ExternalOutput").ap()

    for nm in ("XC", "XA", "XR", "QF", "KF", "KBF", "VBF", "VF", "OF"):
        E[nm] = itn(nm + "i", [NCH, 128, T])
    E["BETAM"] = itn("BETAMi", [8, T])
    E["WGQ"] = itn("WGQi", [FFNCH, 128, D])
    E["WUQ"] = itn("WUQi", [FFNCH, 128, D])
    E["WDQ"] = itn("WDQi", [NCH, 128, FFND])

    sb = {}
    ps = {}

    def sbuf(name, shape, dtype=F32):
        sb[name] = nc.alloc_sbuf_tensor(name, list(shape), dtype).ap()

    def psum(name, shape):
        ps[name] = nc.alloc_psum_tensor(name, list(shape), F32).ap()

    sbuf("arena", [128, 37376])
    sbuf("c_invsc", [128, T])
    sbuf("c_stril", [128, 128])
    sbuf("c_striu", [128, 128])
    sbuf("c_causal", [128, 128])
    sbuf("c_eps", [128, 1])
    sbuf("c_l2eps", [128, 1])
    sbuf("c_zeros", [128, 512])
    sbuf("meanw", [128, 128])
    sbuf("mean256w", [128, 128])
    sbuf("onesw", [128, 128])
    sbuf("c_ident", [128, 128])
    sbuf("c_causalT", [128, 128])
    sbuf("sel8s", [8, 8 * 128])
    sbuf("wbp", [128, 1024])
    for nm in ("g_conv", "b_conv", "g_attn", "b_attn", "g_boom", "b_boom",
               "bq", "bk", "bv", "conv_b", "bu", "normw"):
        sbuf("p_" + nm, [128, NCH])
    sbuf("p_bb", [128, NBOOMCH])
    for nm in ("cqb", "ckb", "cvb"):
        sbuf("p_" + nm, [128, 4 * NCH])
    sbuf("w_beta", [128, NCH * 8])
    sbuf("carry_q", [128, NCH])
    sbuf("carry_pk", [128, NCH])
    sbuf("S_state", [128, NHEAD * 2 * HK])
    sbuf("bm_sb", [8, T])
    sbuf("halo3", [128, NCH * 3 * 4])
    psum("pball", [128, 4096])

    return nc, E, sb, ps


def bank(ps, i, lo=0, n=512, p0=0, pn=128):
    return ps["pball"][p0:p0 + pn, i * 512 + lo:i * 512 + lo + n]



def emit_setup(p: P):
    sb, E = p.sb, p.env
    for nm, valc in (("mean256w", 1.0 / HK), ("onesw", 1.0), ("wbp", 0.0), ("meanw", INVD), ("c_eps", EPS),
                     ("c_l2eps", L2EPS), ("c_zeros", 0.0), ("S_state", 0.0),
                     ("halo3", 0.0), ("carry_q", 0.0), ("carry_pk", 0.0)):
        p.op("vector", lambda v, nm=nm, valc=valc: v.memset(sb[nm][:], valc))
    m_init = p.mark("vector")
    p.dma("sync", sb["c_invsc"][:], E["invsc"][:, :])
    p.dma("sync", sb["c_stril"][:], E["stril_neg"][:, :])
    p.dma("sync", sb["c_striu"][:], E["striu_neg"][:, :])
    p.dma("sync", sb["c_causal"][:], E["causal_incl"][:, :])
    p.dma("sync", sb["c_causalT"][:], E["causalT_incl"][:, :])
    p.dma("sync", sb["c_ident"][:], E["identm"][:, :])
    for r8 in range(8):
        p.dma("sync", sb["sel8s"][:, r8 * 128:(r8 + 1) * 128], E["sel8"][r8, :, :])
    for nm in ("g_conv", "b_conv", "g_attn", "b_attn", "g_boom", "b_boom",
               "bq", "bk", "bv", "conv_b", "bu", "normw"):
        p.dma("sync", sb["p_" + nm][:], E[nm][:, :])
    p.dma("sync", sb["p_bb"][:], E["bb"][:, :])
    for nm in ("cqb", "ckb", "cvb"):
        p.dma("sync", sb["p_" + nm][:], E[nm][:, :])
    p.dma("sync", sb["w_beta"][:], E["wbm"][:, :])
    mi = p.mark("vector")
    p.wait("sync", "c_vector", mi)
    for kc in range(NCH):
        p.dma("sync", sb["wbp"][:, kc * 128: kc * 128 + 8],
              E["wbm"][:, kc * 8:(kc + 1) * 8])
    p.barrier()


def emit_ln(p: P, src, dst, n, nch=NCH, gamma=None, beta=None, sq_scr=None):
    sb, ps = p.sb, p.ps
    assert sq_scr is not None, "emit_ln needs an [128 x nch*n] scratch"

    vm0 = p.val("c_vector")
    p.wait("scalar", "c_vector", vm0)
    p.wait("tensor", "c_vector", vm0)
    sm0 = p.val("c_scalar")
    p.wait("tensor", "c_scalar", sm0)
    p.wait("vector", "c_scalar", sm0)

    for ch in range(nch):
        p.op("scalar", lambda s, ch=ch: s.activation(
            out=sq_scr[:, ch * n:(ch + 1) * n], in_=src[ch], func=AF.Square))
    smark = p.mark("scalar")
    for ch in range(nch):
        p.op("tensor", lambda t, ch=ch: t.matmul(
            bank(ps, 6, 0, n), sb["meanw"][:, :], src[ch],
            start=(ch == 0), stop=(ch == nch - 1)))
    p.wait("tensor", "c_scalar", smark)
    for ch in range(nch):
        p.op("tensor", lambda t, ch=ch: t.matmul(
            bank(ps, 7, 0, n), sb["meanw"][:, :],
            sq_scr[:, ch * n:(ch + 1) * n],
            start=(ch == 0), stop=(ch == nch - 1)))
    tmark = p.mark("tensor")
    p.wait("vector", "c_tensor", tmark)
    p.wait("scalar", "c_tensor", tmark)
    rx = sq_scr[:, 0:n]
    mx = sq_scr[:, n:2 * n]
    p.op("vector", lambda v: v.tensor_copy(out=mx, in_=bank(ps, 6, 0, n)))
    p.op("vector", lambda v: v.tensor_mul(rx, mx, mx))
    vmark = p.op("vector", lambda v: v.tensor_sub(rx, bank(ps, 7, 0, n), rx),
                 inc=True)
    p.wait("scalar", "c_vector", vmark)
    smark = p.op("scalar", lambda s: s.activation(
        out=rx, in_=rx, func=AF.Sqrt, bias=sb["c_eps"][:, 0:1], scale=1.0),
        inc=True)
    p.wait("vector", "c_scalar", smark)
    p.op("vector", lambda v: v.reciprocal(out=rx, in_=rx))
    for ch in range(nch):
        p.op("vector", lambda v, ch=ch: v.tensor_sub(dst[ch], src[ch], mx))
    for ch in range(nch):
        p.op("vector", lambda v, ch=ch: v.tensor_mul(dst[ch], dst[ch], rx))
    if gamma is not None:
        for ch in range(nch):
            p.op("vector", lambda v, ch=ch: v.tensor_scalar(
                out=dst[ch], in0=dst[ch],
                scalar1=sb["p_" + gamma][:, ch:ch + 1],
                scalar2=sb["p_" + beta][:, ch:ch + 1],
                op0=ALU.mult, op1=ALU.add))
    return p.mark("vector")


def proj_1024(p: P, wsb, src, dst_fn, n, bias=None, act=None, banks=(0, 1, 2, 3)):
    sb = p.sb
    nb = len(banks)
    ev = [None] * nb
    p.wait("tensor", "c_scalar", p.val("c_scalar"))
    for oc in range(NCH):
        bk_i = banks[oc % nb]
        if ev[oc % nb] is not None:
            p.wait("tensor", "c_scalar", ev[oc % nb])
        for kc in range(NCH):
            p.op("tensor", lambda t, oc=oc, kc=kc, bk_i=bk_i: t.matmul(
                bank(p.ps, bk_i, 0, n),
                wsb[:, kc * D + oc * 128: kc * D + oc * 128 + 128],
                src[kc], start=(kc == 0), stop=(kc == NCH - 1)))
        tmark = p.mark("tensor")
        p.wait("scalar", "c_tensor", tmark)
        func = act if act is not None else AF.Identity
        if bias is not None:
            ev[oc % nb] = p.op("scalar", lambda s, oc=oc, bk_i=bk_i, func=func: s.activation(
                out=dst_fn(oc), in_=bank(p.ps, bk_i, 0, n), func=func,
                bias=sb["p_" + bias][:, oc:oc + 1], scale=1.0), inc=True)
        else:
            ev[oc % nb] = p.op("scalar", lambda s, oc=oc, bk_i=bk_i, func=func: s.activation(
                out=dst_fn(oc), in_=bank(p.ps, bk_i, 0, n), func=func), inc=True)
    return p.mark("scalar")



def emit_s1(p: P, nstrips=None):
    sb, ps, E = p.sb, p.ps, p.env
    nstrips = DEBUG_S1_STRIPS if nstrips is None else nstrips
    A = sb["arena"]
    n = STRIP
    SEG = n + 3
    OFF_X, OFF_H, OFF_SEG, OFF_CW, OFF_EV = 0, 4096, 8192, 12416, 20608
    for tap in range(4):
        for g in range(4):
            for cc in range(2):
                for oh in range(2):
                    idx = ((tap * 4 + g) * 2 + cc) * 2 + oh
                    p.dma("sync", A[:, OFF_CW + idx * 128: OFF_CW + (idx + 1) * 128],
                          E["cwT"][tap, g, cc * 128:(cc + 1) * 128,
                                   oh * 128:(oh + 1) * 128])
    p.barrier()
    if DEBUG_STAGE == "CWONLY":
        return

    for s in range(nstrips):
        lo = s * n
        xs = [A[:, OFF_X + ch * n: OFF_X + (ch + 1) * n] for ch in range(NCH)]
        hs = [A[:, OFF_H + ch * n: OFF_H + (ch + 1) * n] for ch in range(NCH)]
        seg = [A[:, OFF_SEG + ch * SEG: OFF_SEG + (ch + 1) * SEG]
               for ch in range(NCH)]
        for ch in range(NCH):
            p.dma("sync", xs[ch], E["x0"][ch, :, lo:lo + n])
        dm = p.val("d_sync")
        p.wait_many(("vector", "scalar", "tensor"), "d_sync", dm)
        if DEBUG_S1_PART >= 0:
            for ch in range(NCH):
                if s == 0:
                    p.op("vector", lambda v, ch=ch: v.memset(seg[ch][:, 0:3], 0.0))
                else:
                    p.op("vector", lambda v, ch=ch: v.tensor_copy(
                        out=seg[ch][:, 0:3], in_=seg[ch][:, n:n + 3]))
            p.mark("vector")
        if DEBUG_S1_PART >= 1:
            emit_ln(p, xs, hs, n, sq_scr=A[:, 21632:21632 + NCH * n])
        if DEBUG_S1_PART >= 2:
            emit_ln(p, hs, [sg[:, 3:3 + n] for sg in seg], n,
                    gamma="g_conv", beta="b_conv",
                    sq_scr=A[:, 21632:21632 + NCH * n])
        if DEBUG_S1_PART < 3:
            vmx = p.mark("vector")
            p.wait("sync", "c_vector", vmx)
            src_dbg = hs if DEBUG_S1_PART >= 1 else xs
            tgt_dbg = "xout" if DEBUG_S1_PART == -2 else "XC"
            for ch in range(NCH):
                p.dma("sync", E[tgt_dbg][ch, :, lo:lo + n], src_dbg[ch])
            p.barrier(("vector", "scalar", "tensor", "gpsimd", "sync"))
            continue
        lnm = p.mark("vector")
        p.wait("tensor", "c_vector", lnm)
        ev = [None] * 4
        for oc in range(NCH):
            g, oh = oc // 2, oc % 2
            bk_i = oc % 4
            if ev[bk_i] is not None:
                p.wait("tensor", "c_vector", ev[bk_i])
            for tap in range(4):
                for cc in range(2):
                    idx = ((tap * 4 + g) * 2 + cc) * 2 + oh
                    lhsT = A[:, OFF_CW + idx * 128: OFF_CW + (idx + 1) * 128]
                    rhs = seg[g * 2 + cc][:, tap:tap + n]
                    p.op("tensor", lambda t, lhsT=lhsT, rhs=rhs, bk_i=bk_i,
                         first=(tap == 0 and cc == 0),
                         last=(tap == 3 and cc == 1): t.matmul(
                        bank(ps, bk_i, 0, n), lhsT, rhs,
                        start=first, stop=last))
            tmark = p.mark("tensor")
            p.wait("scalar", "c_tensor", tmark)
            scr = A[:, OFF_EV + (oc % 2) * n: OFF_EV + (oc % 2) * n + n]
            sm = p.op("scalar", lambda s_, oc=oc, bk_i=bk_i, scr=scr: s_.activation(
                out=scr, in_=bank(ps, bk_i, 0, n), func=AF.Identity,
                bias=sb["p_conv_b"][:, oc:oc + 1], scale=1.0), inc=True)
            p.wait("vector", "c_scalar", sm)
            ev[bk_i] = p.op("vector", lambda v, oc=oc, scr=scr: v.tensor_add(
                scr, scr, hs[oc]), inc=True)
            p.wait("gpsimd", "c_vector", ev[bk_i])
            p.dma(STQ, E["XC"][oc, :, lo:lo + n], scr)
            p.wait("scalar", "d_" + STQ, p.val("d_" + STQ))
        p.barrier(("vector", "scalar", "tensor", "gpsimd", "sync"))
    p.barrier()


def emit_proj_stream(p: P, wname, dstl, h2, n, woff, bias=None):
    sb = p.sb
    A = sb["arena"]
    p.wait("tensor", "c_scalar", p.val("c_scalar"))
    t0 = p.val("c_tensor")
    s0 = p.val("c_scalar")
    vslot_mark = [t0, t0]
    ev_v = [s0, s0]
    for oc in range(NCH):
        slot = oc % 2
        if vslot_mark[slot] is not None:
            p.wait("sync", "c_tensor", vslot_mark[slot])
        dmv = p.dma("sync", A[:, woff + slot * 1024: woff + (slot + 1) * 1024],
                    p.env[wname][oc, :, :], sem="d_s%d" % slot)
        p.wait("tensor", "d_s%d" % slot, dmv)
        if ev_v[slot] is not None:
            p.wait("tensor", "c_scalar", ev_v[slot])
        for kc in range(NCH):
            p.op("tensor", lambda t, oc=oc, kc=kc, slot=slot: t.matmul(
                bank(p.ps, slot, 0, n),
                A[:, woff + slot * 1024 + kc * 128:
                  woff + slot * 1024 + (kc + 1) * 128],
                h2[kc], start=(kc == 0), stop=(kc == NCH - 1)))
        vslot_mark[slot] = p.mark("tensor")
        p.wait("scalar", "c_tensor", vslot_mark[slot])
        if bias is not None:
            ev_v[slot] = p.op("scalar", lambda s_, oc=oc, slot=slot: s_.activation(
                out=dstl[oc], in_=bank(p.ps, slot, 0, n), func=AF.Identity,
                bias=sb["p_" + bias][:, oc:oc + 1], scale=1.0), inc=True)
        else:
            ev_v[slot] = p.op("scalar", lambda s_, oc=oc, slot=slot: s_.activation(
                out=dstl[oc], in_=bank(p.ps, slot, 0, n), func=AF.Identity),
                inc=True)
    return p.mark("scalar")



def emit_s2(p: P):
    sb, ps, E = p.sb, p.ps, p.env
    A = sb["arena"]
    n = STRIP
    OFF_XC, OFF_H2, OFF_Q, OFF_K, OFF_V = 0, 4096, 8192, 12288, 16384
    OFF_T1 = OFF_H2
    OFF_WVD = 20480
    p.barrier()

    for s in range(NSTRIP):
        lo = s * n
        xc = [A[:, OFF_XC + ch * n: OFF_XC + (ch + 1) * n] for ch in range(NCH)]
        h2 = [A[:, OFF_H2 + ch * n: OFF_H2 + (ch + 1) * n] for ch in range(NCH)]
        qs = [A[:, OFF_Q + ch * n: OFF_Q + (ch + 1) * n] for ch in range(NCH)]
        ks = [A[:, OFF_K + ch * n: OFF_K + (ch + 1) * n] for ch in range(NCH)]
        vs = [A[:, OFF_V + ch * n: OFF_V + (ch + 1) * n] for ch in range(NCH)]
        for ch in range(NCH):
            p.dma("sync", xc[ch], E["XC"][ch, :, lo:lo + n])
        dm = p.val("d_sync")
        p.wait_many(("vector", "scalar", "tensor"), "d_sync", dm)
        emit_ln(p, xc, h2, n, gamma="g_attn", beta="b_attn",
                sq_scr=A[:, OFF_Q:OFF_Q + NCH * n])
        lnm = p.mark("vector")
        p.wait("tensor", "c_vector", lnm)
        for (wnm, dstl, bnm) in (("wqTe", qs, "bq"), ("wkTe", ks, "bk"),
                                 ("wvTe", vs, "bv")):
            emit_proj_stream(p, wnm, dstl, h2, n, OFF_WVD, bias=bnm)
        pm = p.mark("scalar")
        p.wait("vector", "c_scalar", pm)
        t1 = [A[:, OFF_T1 + ch * n: OFF_T1 + (ch + 1) * n] for ch in range(NCH)]
        for ch in range(NCH):
            p.op("vector", lambda v, ch=ch: v.tensor_tensor_scan(
                out=t1[ch], data0=qs[ch], data1=sb["c_zeros"][:, 0:n],
                initial=sb["carry_q"][:, ch:ch + 1], op0=ALU.add, op1=ALU.add))
            p.op("vector", lambda v, ch=ch: v.tensor_copy(
                out=sb["carry_q"][:, ch:ch + 1], in_=t1[ch][:, n - 1:n]))
            p.op("vector", lambda v, ch=ch: v.tensor_mul(
                t1[ch], t1[ch], sb["c_invsc"][:, lo:lo + n]))
            p.op("vector", lambda v, ch=ch: v.tensor_mul(
                qs[ch], t1[ch], ks[ch]))
            p.op("vector", lambda v, ch=ch: v.tensor_tensor_scan(
                out=ks[ch], data0=qs[ch], data1=sb["c_zeros"][:, 0:n],
                initial=sb["carry_pk"][:, ch:ch + 1], op0=ALU.add, op1=ALU.add))
            p.op("vector", lambda v, ch=ch: v.tensor_copy(
                out=sb["carry_pk"][:, ch:ch + 1], in_=ks[ch][:, n - 1:n]))
            p.op("vector", lambda v, ch=ch: v.tensor_mul(
                ks[ch], ks[ch], sb["c_invsc"][:, lo:lo + n]))
            p.op("vector", lambda v, ch=ch: v.tensor_mul(
                ks[ch], ks[ch], vs[ch]))
            vm = p.op("vector", lambda v, ch=ch: v.tensor_add(
                ks[ch], ks[ch], xc[ch]), inc=True)
            p.wait("gpsimd", "c_vector", vm)
            p.dma(STQ, E["XA"][ch, :, lo:lo + n], ks[ch])
        p.barrier(("vector", "scalar", "tensor", "gpsimd", "sync"))
    p.barrier()



def emit_s3(p: P):
    sb, ps, E = p.sb, p.ps, p.env
    A = sb["arena"]
    n = STRIP
    OFF_XA, OFF_X0, OFF_H3, OFF_U = 0, 4096, 8192, 12288
    OFF_WB, OFF_WU, OFF_EV = 28672, 30720, 32768
    for s in range(NSTRIP):
        lo = s * n
        xa = [A[:, OFF_XA + ch * n: OFF_XA + (ch + 1) * n] for ch in range(NCH)]
        x0s = [A[:, OFF_X0 + ch * n: OFF_X0 + (ch + 1) * n] for ch in range(NCH)]
        h3 = [A[:, OFF_H3 + ch * n: OFF_H3 + (ch + 1) * n] for ch in range(NCH)]
        for ch in range(NCH):
            p.dma("sync", xa[ch], E["XA"][ch, :, lo:lo + n])
            p.dma("sync", x0s[ch], E["x0"][ch, :, lo:lo + n])
        dm = p.val("d_sync")
        p.wait_many(("vector", "scalar", "tensor"), "d_sync", dm)
        emit_ln(p, xa, h3, n, gamma="g_boom", beta="b_boom",
                sq_scr=A[:, 12288:12288 + NCH * n])
        lnm = p.mark("vector")
        p.wait("tensor", "c_vector", lnm)
        evm = [None, None]
        slot_mark = [None, None]
        for e in range(NBOOMCH):
            slot = e % 2
            if slot_mark[slot] is not None:
                p.wait("sync", "c_tensor", slot_mark[slot])
            dmw = p.dma("sync", A[:, OFF_WB + slot * 1024: OFF_WB + slot * 1024 + 1024],
                        E["wbTe"][e, :, :], sem="d_s%d" % slot)
            p.wait("tensor", "d_s%d" % slot, dmw)
            if evm[slot] is not None:
                p.wait("tensor", "c_scalar", evm[slot])
            for kc in range(NCH):
                p.op("tensor", lambda t, kc=kc, slot=slot: t.matmul(
                    bank(ps, slot, 0, n),
                    A[:, OFF_WB + slot * 1024 + kc * 128:
                      OFF_WB + slot * 1024 + (kc + 1) * 128],
                    h3[kc], start=(kc == 0), stop=(kc == NCH - 1)))
            slot_mark[slot] = p.mark("tensor")
            p.wait("scalar", "c_tensor", slot_mark[slot])
            evm[slot] = p.op("scalar", lambda s_, e=e, slot=slot: s_.activation(
                out=A[:, OFF_U + e * n: OFF_U + (e + 1) * n],
                in_=bank(ps, slot, 0, n), func=AF.Gelu,
                bias=sb["p_bb"][:, e:e + 1], scale=1.0), inc=True)
        am = p.mark("scalar")
        p.wait("tensor", "c_scalar", am)
        for grp in range(2):
            t0m = p.val("c_tensor")
            slot_mark = [t0m, t0m]
            for e in range(NBOOMCH):
                slot2 = e % 2
                if slot_mark[slot2] is not None:
                    p.wait("sync", "c_tensor", slot_mark[slot2])
                dmw = p.dma("sync", A[:, OFF_WU + slot2 * 512:
                                      OFF_WU + slot2 * 512 + 512],
                            E["wuT"][e * 128:(e + 1) * 128,
                                     grp * 512:(grp + 1) * 512],
                            sem="d_s%d" % slot2)
                p.wait("tensor", "d_s%d" % slot2, dmw)
                for di in range(4):
                    p.op("tensor", lambda t, e=e, di=di, slot2=slot2: t.matmul(
                        bank(ps, 2 + di, 0, n),
                        A[:, OFF_WU + slot2 * 512 + di * 128:
                          OFF_WU + slot2 * 512 + (di + 1) * 128],
                        A[:, OFF_U + e * n: OFF_U + (e + 1) * n],
                        start=(e == 0), stop=(e == NBOOMCH - 1)))
                slot_mark[slot2] = p.mark("tensor")
            tmb = p.mark("tensor")
            p.wait("scalar", "c_tensor", tmb)
            for di in range(4):
                dc = grp * 4 + di
                scr = A[:, OFF_EV + (di % 2) * n: OFF_EV + (di % 2) * n + n]
                p.wait("scalar", "d_" + STQ, p.val("d_" + STQ))
                sm = p.op("scalar", lambda s_, dc=dc, scr=scr, di=di: s_.activation(
                    out=scr, in_=bank(ps, 2 + di, 0, n), func=AF.Identity,
                    bias=sb["p_bu"][:, dc:dc + 1], scale=1.0), inc=True)
                p.wait("vector", "c_scalar", sm)
                p.op("vector", lambda v, dc=dc, scr=scr: v.tensor_add(
                    scr, scr, xa[dc]))
                vm = p.op("vector", lambda v, dc=dc, scr=scr: v.tensor_add(
                    scr, scr, x0s[dc]), inc=True)
                p.wait("gpsimd", "c_vector", vm)
                p.dma(STQ, E["XR"][dc, :, lo:lo + n], scr)
            p.wait("tensor", "c_scalar", p.val("c_scalar"))
        p.barrier(("vector", "scalar", "tensor", "gpsimd", "sync"))
    p.barrier()





def emit_quant_fm(p: P, tiles, n, scr_off):
    sb = p.sb
    A = sb["arena"]
    sA = A[:, scr_off:scr_off + n]
    sE = A[:, scr_off + n:scr_off + 2 * n]
    sF = A[:, scr_off + 2 * n:scr_off + 3 * n]
    sG = A[:, scr_off + 3 * n:scr_off + 4 * n]
    sH = A[:, scr_off + 4 * n:scr_off + 5 * n]
    sS = A[:, scr_off + 5 * n:scr_off + 6 * n]
    sm = A[:, scr_off + 6 * n:scr_off + 6 * n + n // 8]
    MIDS = (0.25, 0.75, 1.25, 1.75, 2.5, 3.5, 5.0)
    for X in tiles:
        vm0 = p.mark("vector")
        p.wait("scalar", "c_vector", vm0)
        p.wait("gpsimd", "c_vector", vm0)
        p.op("vector", lambda v: v.transpose(out=sA, in_=X))
        p.op("vector", lambda v: v.tensor_reduce(
            out=sm[:, 0:n // 16], in_=sA.rearrange("p (b w) -> p b w", w=16),
            axis=AX.X, op=ALU.max, apply_absolute_value=True))
        p.op("vector", lambda v: v.drain())
        p.op("vector", lambda v: v.tensor_scalar(
            out=sm[:, n // 16:n // 8], in0=sm[:, 0:n // 16],
            scalar1=1.0 / 6.0, scalar2=1e-12, op0=ALU.mult, op1=ALU.max))
        p.op("vector", lambda v: v.drain())
        vrm = p.op("vector", lambda v: v.reciprocal(
            out=sm[:, 0:n // 16], in_=sm[:, n // 16:n // 8]), inc=True)
        p.op("vector", lambda v: v.tensor_copy(
            out=sA.rearrange("p (b w) -> p b w", w=16),
            in_=sm[:, 0:n // 16].unsqueeze(2).broadcast_to(
                [128, n // 16, 16])))
        p.op("vector", lambda v: v.drain())
        p.op("vector", lambda v: v.transpose(out=sE, in_=sA))
        p.op("vector", lambda v: v.tensor_copy(
            out=sA.rearrange("p (b w) -> p b w", w=16),
            in_=sm[:, n // 16:n // 8].unsqueeze(2).broadcast_to(
                [128, n // 16, 16])))
        p.op("vector", lambda v: v.drain())
        vexp = p.op("vector", lambda v: v.transpose(out=sS, in_=sA), inc=True)
        p.wait("scalar", "c_vector", vexp)
        am_ = p.op("scalar", lambda s: s.activation(
            out=sA, in_=X, func=AF.Abs), inc=True)
        p.wait("vector", "c_scalar", am_)
        p.op("vector", lambda v: v.tensor_mul(sE, sA, sE))
        p.op("vector", lambda v: v.tensor_scalar(
            out=sF, in0=sE, scalar1=MIDS[0], scalar2=None, op0=ALU.is_gt))
        p.op("vector", lambda v: v.tensor_scalar(
            out=sG, in0=sE, scalar1=MIDS[4], scalar2=None, op0=ALU.is_gt))
        p.op("vector", lambda v: v.tensor_scalar(
            out=sH, in0=sE, scalar1=MIDS[6], scalar2=2.0,
            op0=ALU.is_gt, op1=ALU.mult))
        for mi in (1, 2, 3):
            p.op("vector", lambda v, mi=mi: v.scalar_tensor_tensor(
                out=sF, in0=sE, scalar=MIDS[mi], in1=sF,
                op0=ALU.is_gt, op1=ALU.add))
        p.op("vector", lambda v: v.scalar_tensor_tensor(
            out=sG, in0=sE, scalar=MIDS[5], in1=sG,
            op0=ALU.is_gt, op1=ALU.add))
        p.op("vector", lambda v: v.scalar_tensor_tensor(
            out=sF, in0=sF, scalar=0.5, in1=sG, op0=ALU.mult, op1=ALU.add))
        vm = p.op("vector", lambda v: v.tensor_add(sF, sF, sH), inc=True)
        p.wait("scalar", "c_vector", vm)
        sgm = p.op("scalar", lambda s: s.activation(
            out=sH, in_=X, func=AF.Sign), inc=True)
        p.wait("vector", "c_scalar", sgm)
        p.op("vector", lambda v: v.tensor_mul(sF, sF, sH))
        p.op("vector", lambda v: v.tensor_mul(X, sF, sS))
    return p.mark("vector")


def emit_w0(p: P):
    sb, E = p.sb, p.env
    A = sb["arena"]
    SCR = 16384
    for (wsrc, wdst) in (("wgTr", "WGQ"), ("wuTr", "WUQ")):
        for cs0 in range(0, FFND, 512):
            n = min(512, FFND - cs0)
            tiles = []
            for kc in range(NCH):
                t_ = A[:, kc * 512: kc * 512 + n]
                p.dma("sync", t_, E[wsrc][kc * 128:(kc + 1) * 128,
                                          cs0:cs0 + n])
                tiles.append(t_)
            p.barrier()
            emit_quant_fm(p, tiles, n, SCR)
            vm = p.mark("vector")
            p.wait(STQ, "c_vector", vm)
            for kc in range(NCH):
                for eb in range(n // 128):
                    e = (cs0 // 128) + eb
                    p.dma(STQ, E[wdst][e, :, kc * 128:(kc + 1) * 128],
                          A[:, kc * 512 + eb * 128: kc * 512 + (eb + 1) * 128])
            p.barrier()
    for cs0 in range(0, D, 512):
        n = 512
        for kg in range(3):
            kcs = list(range(kg * 8, min(FFNCH, (kg + 1) * 8)))
            tiles = []
            for i, kc in enumerate(kcs):
                t_ = A[:, i * 512: i * 512 + n]
                p.dma("sync", t_, E["wdTr"][kc * 128:(kc + 1) * 128,
                                            cs0:cs0 + n])
                tiles.append(t_)
            p.barrier()
            emit_quant_fm(p, tiles, n, SCR)
            vm = p.mark("vector")
            p.wait(STQ, "c_vector", vm)
            for i, kc in enumerate(kcs):
                for db in range(n // 128):
                    dc = (cs0 // 128) + db
                    p.dma(STQ, E["WDQ"][dc, :, kc * 128:(kc + 1) * 128],
                          A[:, i * 512 + db * 128: i * 512 + (db + 1) * 128])
            p.barrier()



def emit_ffn(p: P):
    sb, ps, E = p.sb, p.ps, p.env
    A = sb["arena"]
    n = STRIP
    OFF_XR, OFF_XQ, OFF_H = 0, 4096, 8192
    OFF_WGU, OFF_WD, OFF_EV, OFF_QS = 19456, 23552, 29184, 30720
    for s in range(NSTRIP):
        lo = s * n
        xr = [A[:, OFF_XR + ch * n: OFF_XR + (ch + 1) * n] for ch in range(NCH)]
        xq = [A[:, OFF_XQ + ch * n: OFF_XQ + (ch + 1) * n] for ch in range(NCH)]
        for ch in range(NCH):
            p.dma("sync", xr[ch], E["XR"][ch, :, lo:lo + n])
        dm = p.val("d_sync")
        p.wait_many(("vector", "scalar", "tensor", "gpsimd"), "d_sync", dm)
        emit_ln(p, xr, xq, n, sq_scr=A[:, OFF_H:OFF_H + NCH * n])
        emit_quant_fm(p, xq, n, OFF_QS)
        qm = p.mark("vector")
        p.wait("tensor", "c_vector", qm)
        p.wait("sync", "c_vector", qm)
        if DEBUG_FFN_PART == 0:
            p.wait(STQ, "c_vector", qm)
            p.wait(STQ, "c_scalar", p.val("c_scalar"))
            for ch in range(NCH):
                p.dma(STQ, E["XR"][ch, :, lo:lo + n], xq[ch])
            p.barrier(("vector", "scalar", "tensor", "gpsimd", "sync"))
            continue
        evm = [None, None]
        slot_mark = [None, None]
        for e in range(FFNCH):
            slot = e % 2
            if slot_mark[slot] is not None:
                p.wait("sync", "c_tensor", slot_mark[slot])
            dg = p.dma("sync", A[:, OFF_WGU + slot * 2048: OFF_WGU + slot * 2048 + 1024],
                       E["WGQ"][e, :, :], sem="d_s%d" % slot)
            du = p.dma("sync", A[:, OFF_WGU + slot * 2048 + 1024: OFF_WGU + slot * 2048 + 2048],
                       E["WUQ"][e, :, :], sem="d_s%d" % slot)
            p.wait("tensor", "d_s%d" % slot, du)
            if evm[slot] is not None:
                p.wait("tensor", "c_vector", evm[slot])
            gb, ub = (0, 1) if slot == 0 else (2, 3)
            for kc in range(NCH):
                p.op("tensor", lambda t, kc=kc, slot=slot, gb=gb: t.matmul(
                    bank(ps, gb, 0, n), A[:, OFF_WGU + slot * 2048 + kc * 128:
                                          OFF_WGU + slot * 2048 + (kc + 1) * 128],
                    xq[kc], start=(kc == 0), stop=(kc == NCH - 1)))
            for kc in range(NCH):
                p.op("tensor", lambda t, kc=kc, slot=slot, ub=ub: t.matmul(
                    bank(ps, ub, 0, n), A[:, OFF_WGU + slot * 2048 + 1024 + kc * 128:
                                          OFF_WGU + slot * 2048 + 1024 + (kc + 1) * 128],
                    xq[kc], start=(kc == 0), stop=(kc == NCH - 1)))
            slot_mark[slot] = p.mark("tensor")
            p.wait("scalar", "c_tensor", slot_mark[slot])
            sm = p.op("scalar", lambda s_, e=e, gb=gb, slot=slot: s_.activation(
                out=A[:, OFF_EV + slot * n:OFF_EV + slot * n + n],
                in_=bank(ps, gb, 0, n), func=AF.Silu), inc=True)
            p.wait("vector", "c_scalar", sm)
            evm[slot] = p.op("vector", lambda v, e=e, ub=ub, slot=slot: v.tensor_mul(
                A[:, OFF_H + e * n: OFF_H + (e + 1) * n],
                A[:, OFF_EV + slot * n:OFF_EV + slot * n + n], bank(ps, ub, 0, n)),
                inc=True)
        hm = p.mark("vector")
        p.wait("gpsimd", "c_vector", hm)
        htiles = [A[:, OFF_H + e * n: OFF_H + (e + 1) * n] for e in range(FFNCH)]
        emit_quant_fm(p, htiles, n, OFF_QS)
        qm2 = p.mark("vector")
        p.wait("tensor", "c_vector", qm2)
        p.wait("sync", "c_vector", qm2)
        if DEBUG_FFN_PART == 1:
            p.wait(STQ, "c_vector", qm2)
            p.wait(STQ, "c_scalar", p.val("c_scalar"))
            for ch in range(NCH):
                p.dma(STQ, E["XR"][ch, :, lo:lo + n], htiles[ch])
            p.barrier(("vector", "scalar", "tensor", "gpsimd", "sync"))
            continue
        for grp in range(2 if DEBUG_FFN_PART >= 2 else 0):
            t0m = p.val("c_tensor")
            dslot_mark = [t0m, t0m]
            for di in range(4):
                dc = grp * 4 + di
                slot = di % 2
                if dslot_mark[slot] is not None:
                    p.wait("sync", "c_tensor", dslot_mark[slot])
                dd = p.dma("sync", A[:, OFF_WD + slot * 2816: OFF_WD + (slot + 1) * 2816],
                           E["WDQ"][dc, :, :], sem="d_s%d" % slot)
                p.wait("tensor", "d_s%d" % slot, dd)
                for kc in range(FFNCH):
                    p.op("tensor", lambda t, kc=kc, slot=slot, di=di: t.matmul(
                        bank(ps, 4 + di, 0, n),
                        A[:, OFF_WD + slot * 2816 + kc * 128:
                          OFF_WD + slot * 2816 + (kc + 1) * 128],
                        htiles[kc], start=(kc == 0), stop=(kc == FFNCH - 1)))
                dslot_mark[slot] = p.mark("tensor")
            tmb = p.mark("tensor")
            p.wait("vector", "c_tensor", tmb)
            p.wait("scalar", "c_tensor", tmb)
            for di in range(4 if DEBUG_FFN_PART >= 3 else 0):
                dc = grp * 4 + di
                scr = A[:, OFF_EV + (di % 2) * n: OFF_EV + (di % 2) * n + n]
                p.wait("scalar", "d_" + STQ, p.val("d_" + STQ))
                sm_e = p.op("scalar", lambda s_, di=di, scr=scr: s_.activation(
                    out=scr, in_=bank(ps, 4 + di, 0, n), func=AF.Identity),
                    inc=True)
                p.wait("vector", "c_scalar", sm_e)
                vm = p.op("vector", lambda v, dc=dc, scr=scr: v.tensor_add(
                    scr, scr, xr[dc]), inc=True)
                p.wait(STQ, "c_vector", vm)
                p.dma(STQ, E["XR"][dc, :, lo:lo + n], scr)
            p.wait("tensor", "c_vector", p.val("c_vector"))
        p.barrier(("vector", "scalar", "tensor", "gpsimd", "sync"))
    p.barrier()



def emit_s5a(p: P):
    sb, ps, E = p.sb, p.ps, p.env
    A = sb["arena"]
    n = STRIP
    SEG = n + 3
    OFF_XR, OFF_H = 0, 4096
    OFF_QS, OFF_KS, OFF_VS = 8192, 12416, 16640
    OFF_QC, OFF_KC, OFF_VC = 20864, 24960, 29056
    OFF_WST, OFF_SCR = 33152, 35200
    segs = {"q": OFF_QS, "k": OFF_KS, "v": OFF_VS}
    outs = {"q": OFF_QC, "k": OFF_KC, "v": OFF_VC}
    wname = {"q": "wqbTe", "k": "wkbTe", "v": "wvbTe"}
    cname = {"q": "cqb", "k": "ckb", "v": "cvb"}
    for s in range(NSTRIP):
        lo = s * n
        xr = [A[:, OFF_XR + ch * n: OFF_XR + (ch + 1) * n] for ch in range(NCH)]
        h = [A[:, OFF_H + ch * n: OFF_H + (ch + 1) * n] for ch in range(NCH)]
        for ch in range(NCH):
            p.dma("sync", xr[ch], E["XR"][ch, :, lo:lo + n])
        dm = p.val("d_sync")
        p.wait_many(("vector", "scalar", "tensor", "gpsimd"), "d_sync", dm)
        for ti, tn in enumerate(("q", "k", "v")):
            for ch in range(NCH):
                sg = A[:, segs[tn] + ch * SEG: segs[tn] + (ch + 1) * SEG]
                if s == 0:
                    p.op("vector", lambda v, sg=sg: v.memset(sg[:, 0:3], 0.0))
                else:
                    p.op("vector", lambda v, sg=sg: v.tensor_copy(
                        out=sg[:, 0:3], in_=sg[:, n:n + 3]))
        p.mark("vector")
        emit_ln(p, xr, h, n, sq_scr=A[:, OFF_QC:OFF_QC + NCH * n])
        lnm = p.mark("vector")
        p.wait("tensor", "c_vector", lnm)
        for kc in range(NCH):
            p.op("tensor", lambda t, kc=kc: t.matmul(
                bank(ps, 7, 0, n), sb["wbp"][:, kc * 128:(kc + 1) * 128],
                h[kc], start=(kc == 0), stop=(kc == NCH - 1)))
        bmm = p.mark("tensor")
        p.wait("scalar", "c_tensor", bmm)
        p.op("scalar", lambda s_: s_.activation(
            out=sb["bm_sb"][:, lo:lo + n], in_=bank(ps, 7, 0, n, 0, 8),
            func=AF.Sigmoid), inc=True)
        vslot_mark = [None, None]
        ev_v = [None, None]
        for tn in ("q", "k", "v"):
            p.wait("tensor", "c_scalar", p.val("c_scalar"))
            for oc in range(NCH):
                slot = oc % 2
                if vslot_mark[slot] is not None:
                    p.wait("sync", "c_tensor", vslot_mark[slot])
                dmv = p.dma("sync", A[:, OFF_WST + slot * 1024: OFF_WST + (slot + 1) * 1024],
                            E[wname[tn]][oc, :, :], sem="d_s%d" % slot)
                p.wait("tensor", "d_s%d" % slot, dmv)
                if ev_v[slot] is not None:
                    p.wait("tensor", "c_scalar", ev_v[slot])
                for kc in range(NCH):
                    p.op("tensor", lambda t, oc=oc, kc=kc, slot=slot: t.matmul(
                        bank(ps, slot, 0, n),
                        A[:, OFF_WST + slot * 1024 + kc * 128:
                          OFF_WST + slot * 1024 + (kc + 1) * 128],
                        h[kc], start=(kc == 0), stop=(kc == NCH - 1)))
                vslot_mark[slot] = p.mark("tensor")
                p.wait("scalar", "c_tensor", vslot_mark[slot])
                sg = A[:, segs[tn] + oc * SEG: segs[tn] + (oc + 1) * SEG]
                ev_v[slot] = p.op("scalar", lambda s_, sg=sg, slot=slot: s_.activation(
                    out=sg[:, 3:3 + n], in_=bank(ps, slot, 0, n),
                    func=AF.Identity), inc=True)
            pm = p.mark("scalar")
            p.wait("vector", "c_scalar", pm)
            for ch in range(NCH):
                sg = A[:, segs[tn] + ch * SEG: segs[tn] + (ch + 1) * SEG]
                oc_ = A[:, outs[tn] + ch * n: outs[tn] + (ch + 1) * n]
                p.op("vector", lambda v, sg=sg, oc_=oc_, ch=ch, tn=tn: v.tensor_scalar(
                    out=oc_, in0=sg[:, 3:3 + n],
                    scalar1=sb["p_" + cname[tn]][:, 3 * 8 + ch:3 * 8 + ch + 1],
                    scalar2=None, op0=ALU.mult))
                for tap in range(3):
                    p.op("vector", lambda v, sg=sg, oc_=oc_, ch=ch, tap=tap, tn=tn:
                         v.scalar_tensor_tensor(
                        out=oc_, in0=sg[:, tap:tap + n],
                        scalar=sb["p_" + cname[tn]][:, tap * 8 + ch:tap * 8 + ch + 1],
                        in1=oc_, op0=ALU.mult, op1=ALU.add))
            cm = p.mark("vector")
            p.wait("scalar", "c_vector", cm)
            for ch in range(NCH):
                oc_ = A[:, outs[tn] + ch * n: outs[tn] + (ch + 1) * n]
                p.op("scalar", lambda s_, oc_=oc_: s_.activation(
                    out=oc_, in_=oc_, func=AF.Silu))
            sm2 = p.mark("scalar")
            p.wait("tensor", "c_scalar", sm2)
            p.wait("vector", "c_scalar", sm2)
            if tn in ("q", "k"):
                for hh in range(NHEAD):
                    c0 = A[:, outs[tn] + (2 * hh) * n: outs[tn] + (2 * hh) * n + n]
                    c1 = A[:, outs[tn] + (2 * hh + 1) * n: outs[tn] + (2 * hh + 1) * n + n]
                    sq0 = A[:, OFF_SCR:OFF_SCR + n]
                    sq1 = A[:, OFF_SCR + n:OFF_SCR + 2 * n]
                    sqm = p.op("scalar", lambda s_, c0=c0, sq0=sq0: s_.activation(
                        out=sq0, in_=c0, func=AF.Square))
                    sqm = p.op("scalar", lambda s_, c1=c1, sq1=sq1: s_.activation(
                        out=sq1, in_=c1, func=AF.Square), inc=True)
                    p.wait("tensor", "c_scalar", sqm)
                    p.op("tensor", lambda t, sq0=sq0: t.matmul(
                        bank(ps, 6, 0, n), sb["onesw"][:, :], sq0,
                        start=True, stop=False))
                    p.op("tensor", lambda t, sq1=sq1: t.matmul(
                        bank(ps, 6, 0, n), sb["onesw"][:, :], sq1,
                        start=False, stop=True))
                    tm6 = p.mark("tensor")
                    p.wait("scalar", "c_tensor", tm6)
                    rr = A[:, OFF_SCR + 2 * n:OFF_SCR + 3 * n]
                    srm = p.op("scalar", lambda s_, rr=rr: s_.activation(
                        out=rr, in_=bank(ps, 6, 0, n), func=AF.Sqrt,
                        bias=sb["c_l2eps"][:, 0:1], scale=1.0), inc=True)
                    p.wait("vector", "c_scalar", srm)
                    p.op("vector", lambda v, rr=rr: v.reciprocal(out=rr, in_=rr))
                    p.op("vector", lambda v: v.drain())
                    p.op("vector", lambda v, c0=c0, rr=rr: v.tensor_mul(c0, c0, rr))
                    vnm = p.op("vector", lambda v, c1=c1, rr=rr: v.tensor_mul(
                        c1, c1, rr), inc=True)
                    p.wait("tensor", "c_vector", vnm)
                    p.wait("scalar", "c_vector", vnm)
        smA = p.val("c_scalar")
        p.wait("tensor", "c_scalar", smA)
        vq = p.val("c_vector")
        p.wait("tensor", "c_vector", vq)
        for hh in range(NHEAD):
            p.op("tensor", lambda t, hh=hh: t.matmul(
                bank(ps, 5, 0, n), sb["sel8s"][:, hh * 128:(hh + 1) * 128],
                sb["bm_sb"][0:8, lo:lo + n], start=True, stop=True))
            tm5 = p.mark("tensor")
            p.wait("vector", "c_tensor", tm5)
            for dc in range(2):
                ch = 2 * hh + dc
                kcv = A[:, OFF_KC + ch * n: OFF_KC + (ch + 1) * n]
                vcv = A[:, OFF_VC + ch * n: OFF_VC + (ch + 1) * n]
                kbv = A[:, OFF_H + ch * n: OFF_H + (ch + 1) * n]
                vbv = A[:, OFF_XR + ch * n: OFF_XR + (ch + 1) * n]
                p.op("vector", lambda v, kbv=kbv, kcv=kcv: v.tensor_mul(
                    kbv, kcv, bank(ps, 5, 0, n)))
                p.op("vector", lambda v, vbv=vbv, vcv=vcv: v.tensor_mul(
                    vbv, vcv, bank(ps, 5, 0, n)))
            vbm = p.mark("vector")
            p.wait("tensor", "c_vector", vbm)
        fm = p.mark("vector")
        p.wait(STQ, "c_vector", fm)
        p.wait(STQ, "c_scalar", p.val("c_scalar"))
        for ch in range(NCH):
            p.dma(STQ, E["QF"][ch, :, lo:lo + n],
                  A[:, OFF_QC + ch * n: OFF_QC + (ch + 1) * n])
            p.dma(STQ, E["KF"][ch, :, lo:lo + n],
                  A[:, OFF_KC + ch * n: OFF_KC + (ch + 1) * n])
            p.dma(STQ, E["VF"][ch, :, lo:lo + n],
                  A[:, OFF_VC + ch * n: OFF_VC + (ch + 1) * n])
            p.dma(STQ, E["KBF"][ch, :, lo:lo + n],
                  A[:, OFF_H + ch * n: OFF_H + (ch + 1) * n])
            p.dma(STQ, E["VBF"][ch, :, lo:lo + n],
                  A[:, OFF_XR + ch * n: OFF_XR + (ch + 1) * n])
        p.barrier(("vector", "scalar", "tensor", "gpsimd", "sync"))
    p.barrier()



def emit_s5bc(p: P):
    sb, ps, E = p.sb, p.ps, p.env
    A = sb["arena"]
    c = CHUNK
    OQ, OK, OKB, OV = 0, 1024, 2048, 3072
    OKT, OVT = 4096, 4352
    OA, OAT, OATT = 4608, 4736, 4864
    OP, OQQ, OY = 4992, 5760, 6528
    OOA, OXR, OSC, OWO = 7296, 8320, 9344, 10240
    for oc in range(NCH):
        p.dma("sync", A[:, OWO + oc * D: OWO + (oc + 1) * D],
              E["woTe"][oc, :, :])
    p.barrier()
    for i in range(NCHUNKS):
        t0 = i * c
        for ch in range(NCH):
            p.dma("sync", A[:, OQ + ch * c: OQ + (ch + 1) * c],
                  E["QF"][ch, :, t0:t0 + c])
            p.dma("sync", A[:, OK + ch * c: OK + (ch + 1) * c],
                  E["KF"][ch, :, t0:t0 + c])
            p.dma("sync", A[:, OKB + ch * c: OKB + (ch + 1) * c],
                  E["KBF"][ch, :, t0:t0 + c])
            p.dma("sync", A[:, OV + ch * c: OV + (ch + 1) * c],
                  E["VF"][ch, :, t0:t0 + c])
            p.dma("sync", A[:, OXR + ch * c: OXR + (ch + 1) * c],
                  E["XR"][ch, :, t0:t0 + c])
        dm = p.val("d_sync")
        p.wait_many(("vector", "scalar", "tensor"), "d_sync", dm)
        osc_guard = None
        for hh in range(NHEAD):
            q_ = [A[:, OQ + (2 * hh + dc) * c: OQ + (2 * hh + dc + 1) * c]
                  for dc in range(2)]
            k_ = [A[:, OK + (2 * hh + dc) * c: OK + (2 * hh + dc + 1) * c]
                  for dc in range(2)]
            kb_ = [A[:, OKB + (2 * hh + dc) * c: OKB + (2 * hh + dc + 1) * c]
                   for dc in range(2)]
            Sh = [sb["S_state"][:, (hh * 2 + dc) * HK:(hh * 2 + dc) * HK + HK]
                  for dc in range(2)]
            for dc in range(2):
                p.op("tensor", lambda t, dc=dc: t.matmul(
                    bank(ps, 0, 0, c), kb_[dc], k_[dc],
                    start=(dc == 0), stop=(dc == 1)))
            for dc in range(2):
                p.op("tensor", lambda t, dc=dc: t.matmul(
                    bank(ps, 1, 0, c), k_[dc], kb_[dc],
                    start=(dc == 0), stop=(dc == 1)))
            for dc in range(2):
                p.op("tensor", lambda t, dc=dc: t.matmul(
                    bank(ps, 2, 0, c), k_[dc], q_[dc],
                    start=(dc == 0), stop=(dc == 1)))
            tm = p.mark("tensor")
            p.wait("vector", "c_tensor", tm)
            p.op("vector", lambda v: v.tensor_mul(A[:, OA:OA + c],
                                                  bank(ps, 0, 0, c),
                                                  sb["c_stril"][:]))
            p.op("vector", lambda v: v.tensor_mul(A[:, OAT:OAT + c],
                                                  bank(ps, 1, 0, c),
                                                  sb["c_striu"][:]))
            vm = p.op("vector", lambda v: v.tensor_mul(A[:, OATT:OATT + c],
                                                       bank(ps, 2, 0, c),
                                                       sb["c_causalT"][:]),
                      inc=True)
            p.wait("tensor", "c_vector", vm)
            for dc in range(2):
                p.op("tensor", lambda t, dc=dc: t.transpose(
                    bank(ps, 3 + dc, 0, c), k_[dc], sb["c_ident"][:]))
            ttm = p.mark("tensor")
            p.wait("vector", "c_tensor", ttm)
            for dc in range(2):
                p.op("vector", lambda v, dc=dc: v.tensor_copy(
                    out=A[:, OKT + dc * 128:OKT + (dc + 1) * 128],
                    in_=bank(ps, 3 + dc, 0, c)))
            vtm = p.mark("vector")
            p.wait("tensor", "c_vector", vtm)
            if osc_guard is not None:
                p.wait("sync", "c_vector", osc_guard[0])
                p.wait("sync", "c_scalar", osc_guard[1])
            for dc in range(2):
                p.dma("sync", A[:, OSC + dc * c: OSC + (dc + 1) * c],
                      E["VBF"][2 * hh + dc, :, t0:t0 + c], sem="d_s0")
            dvb = p.val("d_s0")
            p.wait("tensor", "d_s0", dvb)
            for dc in range(2):
                p.op("tensor", lambda t, dc=dc: t.transpose(
                    bank(ps, 3 + dc, 0, c),
                    A[:, OSC + dc * c: OSC + (dc + 1) * c], sb["c_ident"][:]))
            ttm2 = p.mark("tensor")
            p.wait("vector", "c_tensor", ttm2)
            for dc in range(2):
                p.op("vector", lambda v, dc=dc: v.tensor_copy(
                    out=A[:, OVT + dc * 128:OVT + (dc + 1) * 128],
                    in_=bank(ps, 3 + dc, 0, c)))
            vtm2 = p.mark("vector")
            p.wait("tensor", "c_vector", vtm2)
            Plist = [A[:, OA:OA + c]]
            Qlist = [A[:, OAT:OAT + c]]
            for j in range(1, 7):
                p.op("tensor", lambda t, j=j: t.matmul(
                    bank(ps, 3, 0, c), Qlist[j - 1], Plist[j - 1],
                    start=True, stop=True))
                p.op("tensor", lambda t, j=j: t.matmul(
                    bank(ps, 4, 0, c), Plist[j - 1], Qlist[j - 1],
                    start=True, stop=True))
                tsq = p.mark("tensor")
                p.wait("vector", "c_tensor", tsq)
                Pj = A[:, OP + (j - 1) * c: OP + j * c]
                Qj = A[:, OQQ + (j - 1) * c: OQQ + j * c]
                p.op("vector", lambda v, Pj=Pj: v.tensor_copy(
                    out=Pj, in_=bank(ps, 3, 0, c)))
                vsq = p.op("vector", lambda v, Qj=Qj: v.tensor_copy(
                    out=Qj, in_=bank(ps, 4, 0, c)), inc=True)
                p.wait("tensor", "c_vector", vsq)
                Plist.append(Pj)
                Qlist.append(Qj)
            for dc in range(2):
                p.op("tensor", lambda t, dc=dc: t.matmul(
                    bank(ps, 5, 0, HK), kb_[dc], Sh[dc],
                    start=(dc == 0), stop=(dc == 1)))
            tmy = p.mark("tensor")
            p.wait("vector", "c_tensor", tmy)
            Rbufs = [A[:, OY + 256:OY + 512], A[:, OY + 512:OY + 768]]
            vmy = p.op("vector", lambda v: v.tensor_sub(
                Rbufs[0], A[:, OVT:OVT + 256], bank(ps, 5, 0, HK)), inc=True)
            p.wait("tensor", "c_vector", vmy)
            Rcur = Rbufs[0]
            for j in range(7):
                p.op("tensor", lambda t, j=j, Rcur=Rcur: t.matmul(
                    bank(ps, 5, 0, HK), Qlist[j], Rcur, start=True, stop=True))
                tcj = p.mark("tensor")
                p.wait("vector", "c_tensor", tcj)
                Rnext = Rbufs[(j + 1) % 2]
                vcj = p.op("vector", lambda v, Rcur=Rcur, Rnext=Rnext:
                           v.tensor_add(Rnext, bank(ps, 5, 0, HK), Rcur),
                           inc=True)
                p.wait("tensor", "c_vector", vcj)
                Rcur = Rnext
            unew = Rcur
            for mc, bk_i in ((0, 6), (1, 2)):
                for dc in range(2):
                    p.op("tensor", lambda t, mc=mc, dc=dc, bk_i=bk_i: t.matmul(
                        bank(ps, bk_i, 0, c),
                        Sh[dc][:, mc * 128:(mc + 1) * 128], q_[dc],
                        start=(dc == 0), stop=False))
                p.op("tensor", lambda t, mc=mc, bk_i=bk_i, unew=unew: t.matmul(
                    bank(ps, bk_i, 0, c), unew[:, mc * 128:(mc + 1) * 128],
                    A[:, OATT:OATT + c], start=False, stop=True))
            tmo = p.mark("tensor")
            for dc in range(2):
                p.op("tensor", lambda t, dc=dc, unew=unew: t.matmul(
                    bank(ps, 3 + dc, 256, HK),
                    A[:, OKT + dc * 128: OKT + (dc + 1) * 128], unew,
                    start=True, stop=True))
            tds = p.mark("tensor")
            p.wait("vector", "c_tensor", tds)
            for dc in range(2):
                p.op("vector", lambda v, dc=dc: v.tensor_add(
                    Sh[dc], Sh[dc], bank(ps, 3 + dc, 256, HK)))
            p.op("tensor", lambda t, hh=hh: t.matmul(
                bank(ps, 5, 256, c), sb["sel8s"][:, (4 + hh) * 128:(5 + hh) * 128],
                sb["bm_sb"][0:8, t0:t0 + c], start=True, stop=True))
            tmx = p.mark("tensor")
            p.wait("vector", "c_tensor", tmx)
            for mc, bk_i in ((0, 6), (1, 2)):
                vv = A[:, OV + (2 * hh + mc) * c: OV + (2 * hh + mc + 1) * c]
                t1 = A[:, OSC:OSC + c]
                p.op("vector", lambda v, bk_i=bk_i, vv=vv, t1=t1: v.tensor_sub(
                    t1, bank(ps, bk_i, 0, c), vv))
                p.op("vector", lambda v: v.drain())
                p.op("vector", lambda v, t1=t1: v.tensor_mul(
                    t1, t1, bank(ps, 5, 256, c)))
                p.op("vector", lambda v: v.drain())
                p.op("vector", lambda v, t1=t1, vv=vv, mc=mc: v.tensor_add(
                    A[:, OSC + (1 + mc) * c: OSC + (2 + mc) * c], t1, vv))
            vg = p.mark("vector")
            p.wait("scalar", "c_vector", vg)
            for mc in range(2):
                p.op("scalar", lambda s_, mc=mc: s_.activation(
                    out=A[:, OSC + (3 + mc) * c: OSC + (4 + mc) * c],
                    in_=A[:, OSC + (1 + mc) * c: OSC + (2 + mc) * c],
                    func=AF.Square))
            sqm = p.mark("scalar")
            p.wait("tensor", "c_scalar", sqm)
            for mc in range(2):
                p.op("tensor", lambda t, mc=mc: t.matmul(
                    bank(ps, 7, 0, c), sb["mean256w"][:, :],
                    A[:, OSC + (3 + mc) * c: OSC + (4 + mc) * c],
                    start=(mc == 0), stop=(mc == 1)))
            trm = p.mark("tensor")
            p.wait("scalar", "c_tensor", trm)
            rr = A[:, OSC + 5 * c: OSC + 6 * c]
            srm = p.op("scalar", lambda s_, rr=rr: s_.activation(
                out=rr, in_=bank(ps, 7, 0, c), func=AF.Sqrt,
                bias=sb["c_eps"][:, 0:1], scale=1.0), inc=True)
            p.wait("vector", "c_scalar", srm)
            p.op("vector", lambda v, rr=rr: v.reciprocal(out=rr, in_=rr))
            p.op("vector", lambda v: v.drain())
            for mc in range(2):
                ch = 2 * hh + mc
                oT = A[:, OOA + ch * c: OOA + (ch + 1) * c]
                p.op("vector", lambda v, oT=oT, mc=mc, rr=rr: v.tensor_mul(
                    oT, A[:, OSC + (1 + mc) * c: OSC + (2 + mc) * c], rr))
                p.op("vector", lambda v: v.drain())
                p.op("vector", lambda v, oT=oT, ch=ch: v.tensor_scalar(
                    out=oT, in0=oT, scalar1=sb["p_normw"][:, ch:ch + 1],
                    scalar2=None, op0=ALU.mult))
            vrm = p.mark("vector")
            p.wait("tensor", "c_vector", vrm)
            osc_guard = (vrm, p.val("c_scalar"))
        for ocb in range(NCH):
            for kc in range(NCH):
                p.op("tensor", lambda t, ocb=ocb, kc=kc: t.matmul(
                    bank(ps, ocb, 0, c),
                    A[:, OWO + ocb * D + kc * 128: OWO + ocb * D + (kc + 1) * 128],
                    A[:, OOA + kc * c: OOA + (kc + 1) * c],
                    start=(kc == 0), stop=(kc == NCH - 1)))
        two = p.mark("tensor")
        p.wait("vector", "c_tensor", two)
        p.wait("scalar", "c_tensor", two)
        for ocb in range(NCH):
            slot = ocb % 2
            scr = A[:, OSC + slot * c: OSC + slot * c + c]
            if ocb >= 2:
                p.wait("scalar", "d_" + STQ, p.val("d_" + STQ))
            sm_w = p.op("scalar", lambda s_, ocb=ocb, scr=scr: s_.activation(
                out=scr, in_=bank(ps, ocb, 0, c), func=AF.Identity), inc=True)
            p.wait("vector", "c_scalar", sm_w)
            vwo = p.op("vector", lambda v, ocb=ocb, scr=scr: v.tensor_add(
                scr, scr, A[:, OXR + ocb * c: OXR + (ocb + 1) * c]), inc=True)
            p.wait(STQ, "c_vector", vwo)
            p.dma(STQ, E["XR"][ocb, :, t0:t0 + c], scr)
        p.barrier(("vector", "scalar", "tensor", "sync", "gpsimd"))
    p.barrier()



def emit_tail_copy(p: P, src_name):
    sb, E = p.sb, p.env
    A = sb["arena"]
    n = STRIP
    idx = 0
    st_count = [0, 0]
    for s in range(NSTRIP):
        lo = s * n
        for ch in range(NCH):
            slot = idx % 2
            buf = A[:, slot * n:(slot + 1) * n]
            if st_count[slot]:
                p.wait("sync", "d_gpsimd", p.val("d_gpsimd"))
            dmn = p.dma("sync", buf, E[src_name][ch, :, lo:lo + n],
                        sem="d_s%d" % slot)
            p.wait("gpsimd", "d_s%d" % slot, dmn)
            st_count[slot] = p.dma("gpsimd", E["xout"][ch, :, lo:lo + n], buf)
            idx += 1
    p.wait("gpsimd", "d_gpsimd", p.val("d_gpsimd"))
    p.barrier()


def emit_all(p: P):
    if DEBUG_NO_SETUP:
        p.barrier()
    else:
        emit_setup(p)
    if DEBUG_STAGE == "SETUP":
        emit_tail_copy(p, "x0")
        return
    emit_s1(p)
    if DEBUG_S1_PART == -2:
        p.wait("gpsimd", "d_sync", p.val("d_sync"))
        p.barrier()
        return
    if DEBUG_STAGE == "CWONLY":
        emit_tail_copy(p, "x0")
        return
    if DEBUG_STAGE == "XC":
        emit_tail_copy(p, "XC")
        return
    emit_s2(p)
    if DEBUG_STAGE == "XA":
        emit_tail_copy(p, "XA")
        return
    emit_s3(p)
    if DEBUG_STAGE == "XR":
        emit_tail_copy(p, "XR")
        return
    emit_w0(p)
    if DEBUG_STAGE == "W0":
        emit_tail_copy(p, "XR")
        return
    emit_ffn(p)
    if DEBUG_STAGE == "F1":
        emit_tail_copy(p, "XR")
        return
    emit_s5a(p)
    emit_s5bc(p)
    if DEBUG_STAGE == "AB":
        emit_tail_copy(p, "XR")
        return
    emit_ffn(p)
    emit_tail_copy(p, "XR")


def build():
    nc, env, sb, ps = build_program()
    with (
        nc.semaphore("d_s0") as sp0,
        nc.semaphore("d_s1") as sp1,
        nc.semaphore("d_sync") as s1,
        nc.semaphore("d_gpsimd") as s2,
        nc.semaphore("d_scalar") as s3,
        nc.semaphore("c_sync") as s4,
        nc.semaphore("c_vector") as s5,
        nc.semaphore("c_scalar") as s6,
        nc.semaphore("c_tensor") as s7,
        nc.semaphore("c_gpsimd") as s8,
        nc.Block() as block,
    ):
        sems = {"d_s0": sp0, "d_s1": sp1,
                "d_sync": s1, "d_gpsimd": s2, "d_scalar": s3, "c_sync": s4,
                "c_vector": s5, "c_scalar": s6, "c_tensor": s7,
                "c_gpsimd": s8}

        @block.sync
        def _(sync):
            emit_all(P(nc, "sync", sync, sems, sb, ps, env))

        @block.vector
        def _(vector):
            emit_all(P(nc, "vector", vector, sems, sb, ps, env))

        @block.scalar
        def _(scalar):
            emit_all(P(nc, "scalar", scalar, sems, sb, ps, env))

        @block.tensor
        def _(tensor):
            emit_all(P(nc, "tensor", tensor, sems, sb, ps, env))

        @block.gpsimd
        def _(gpsimd):
            emit_all(P(nc, "gpsimd", gpsimd, sems, sb, ps, env))

    return nc



def _octile(W):
    return np.ascontiguousarray(
        W.T.reshape(NCH, 128, NCH, 128).transpose(2, 1, 0, 3)
        .reshape(NCH, 128, D))


def _host_inputs(inputs, core):
    b = core % 4
    g = inputs
    x0 = np.ascontiguousarray(
        np.transpose(g["L0_attn_a_hidden_states"][b]).reshape(NCH, 128, T))

    def colchunks(v):
        return np.ascontiguousarray(v.reshape(NCH, 128).T)

    tril = np.tril(np.ones((128, 128), np.float32), -1)
    triu = np.triu(np.ones((128, 128), np.float32), 1)
    causal = np.tril(np.ones((128, 128), np.float32), 0)
    inv = (1.0 / np.arange(1, T + 1, dtype=np.float64)).astype(np.float32)
    m = {
        "x0": x0,
        "wqTe": _octile(g["L0_attn_a_Wq"]),
        "wkTe": _octile(g["L0_attn_a_Wk"]),
        "wvTe": _octile(g["L0_attn_a_Wv"]),
        "bq": colchunks(g["L0_attn_a_bq"]),
        "bk": colchunks(g["L0_attn_a_bk"]),
        "bv": colchunks(g["L0_attn_a_bv"]),
        "g_conv": colchunks(g["L0_attn_a_ln_conv_g"]),
        "b_conv": colchunks(g["L0_attn_a_ln_conv_b"]),
        "g_attn": colchunks(g["L0_attn_a_ln_attn_g"]),
        "b_attn": colchunks(g["L0_attn_a_ln_attn_b"]),
        "g_boom": colchunks(g["L0_attn_a_ln_boom_g"]),
        "b_boom": colchunks(g["L0_attn_a_ln_boom_b"]),
        "conv_b": colchunks(g["L0_attn_a_conv_b"]),
        "cwT": np.ascontiguousarray(
            g["L0_attn_a_conv_w"].reshape(4, 256, 256, 4)
            .transpose(3, 0, 2, 1)),
        "bu": colchunks(g["L0_attn_a_bu"]),
        "bb": np.ascontiguousarray(g["L0_attn_a_bb"].reshape(NBOOMCH, 128).T),
        "wbTe": np.ascontiguousarray(
            g["L0_attn_a_Wb"].T.reshape(NCH, 128, NBOOMCH, 128)
            .transpose(2, 1, 0, 3).reshape(NBOOMCH, 128, D)),
        "wuT": np.ascontiguousarray(g["L0_attn_a_Wu"].T),
        "wqbTe": _octile(g["L0_attn_b_Wq"]),
        "wkbTe": _octile(g["L0_attn_b_Wk"]),
        "wvbTe": _octile(g["L0_attn_b_Wv"]),
        "cqb": np.ascontiguousarray(
            g["L0_attn_b_conv_q"].T.reshape(4 * NCH, 128).T),
        "ckb": np.ascontiguousarray(
            g["L0_attn_b_conv_k"].T.reshape(4 * NCH, 128).T),
        "cvb": np.ascontiguousarray(
            g["L0_attn_b_conv_v"].T.reshape(4 * NCH, 128).T),
        "wbm": np.ascontiguousarray(
            np.concatenate([g["L0_attn_b_Wb"].T.reshape(NCH, 128, 4),
                            g["L0_attn_b_Wmix"].T.reshape(NCH, 128, 4)],
                           axis=2).transpose(1, 0, 2).reshape(128, NCH * 8)),
        "normw": np.ascontiguousarray(
            np.tile(g["L0_attn_b_norm_w"], NHEAD).reshape(NCH, 128).T),
        "woTe": _octile(g["L0_attn_b_Wo"]),
        "wgTr": np.ascontiguousarray(g["L0_ffn_w_gate"].T),
        "wuTr": np.ascontiguousarray(g["L0_ffn_w_up"].T),
        "wdTr": np.ascontiguousarray(g["L0_ffn_w_down"].T),
        "invsc": np.ascontiguousarray(np.broadcast_to(inv, (128, T))),
        "stril_neg": np.ascontiguousarray(-tril),
        "striu_neg": np.ascontiguousarray(-triu),
        "causal_incl": np.ascontiguousarray(causal),
        "causalT_incl": np.ascontiguousarray(causal.T),
        "identm": np.eye(128, dtype=np.float32),
        "sel8": np.ascontiguousarray(
            np.eye(8, dtype=np.float32)[:, :, None]
            * np.ones((1, 1, 128), np.float32)).reshape(8, 8, 128),
    }
    return m


_NC_CACHE = [None]


def kernel(**inputs):
    if _NC_CACHE[0] is None:
        _NC_CACHE[0] = build()
    nc = _NC_CACHE[0]
    in_maps = [_host_inputs(inputs, c) for c in range(8)]
    res = run_bass_kernel_spmd(nc, in_maps, core_ids=list(range(8)))
    out = np.empty((4, T, D), np.float32)
    for b in range(4):
        xo = res.results[b]["xout"]
        out[b] = xo.reshape(D, T).T
    return out
